# version 25
# baseline (speedup 1.0000x reference)
import numpy as np
import ml_dtypes
import concourse.bass as bass
import concourse.mybir as mybir
from concourse.bass_utils import run_bass_kernel_spmd
from contextlib import ExitStack

F32 = mybir.dt.float32
BF16 = mybir.dt.bfloat16
I32 = mybir.dt.int32
AF = mybir.ActivationFunctionType
ALU = mybir.AluOpType
AX = mybir.AxisListType

MAGIC = 12582912.0
RA_, RB_, RC_ = 0.26926346465863016, -1.1080260091717, 1.8587355389354314
NCHUNK = 32
D = 1024
NH = 16

XS = [0, 1024, 2048, 3072]
CEN = [4096, 5120]
T1_ = 6144
GPRE = 7168
GS = [8192, 9216]
OUTS = [10240, 11264]
RSB = 12288
SCR_COLS = 13312
WST = OUTS

XN = [0, 1024]
XNT = [2048, 3072]
QB = [4096, 5120]
QLT = [6144, 7168]
KT = [8192, 9216]
VS = [10240, 11280, 12320]
QTF = 13360
QTM = [14384, 16432]
QROW, KROW = 18480, 19504
BFP_COLS = 20560

ERAW = 0
EBF = 4096
AO = [8192, 9216]
AOT = [10240, 11264]
BFP2_COLS = 12288
QRHS_T = 0

SUMC, NEGM, AMAX, RECA, SQS0, LNV, NT1, NT2 = range(8)
RSTD = [8, 9]
LS = [10, 11]
SQS = [SQS0, 12]
QRA = [13, 14]
CMG, CNMG = 32, 33
AMX2 = [34, 35]
RSUM = 16
RSTL = 40


class Ledger:
    def __init__(self):
        self.count = {}
        self.tag = {}
        self.emit = False

    def inc(self, sem_name, n, tag=None):
        c = self.count.get(sem_name, 0) + n
        self.count[sem_name] = c
        if tag is not None:
            self.tag[(sem_name, tag)] = c
        return c


def build(nchunk=NCHUNK):
    nc = bass.Bass("TRN2", target_bir_lowering=False)
    x_h = nc.dram_tensor("x", [2, 2048, D], F32, kind="ExternalInput").ap()
    wqt_h = nc.dram_tensor("wqt", [D, D], F32, kind="ExternalInput").ap()
    wkt_h = nc.dram_tensor("wkt", [D, D], F32, kind="ExternalInput").ap()
    wvt_h = nc.dram_tensor("wvt", [D, D], F32, kind="ExternalInput").ap()
    wot_h = nc.dram_tensor("wot", [D, D], F32, kind="ExternalInput").ap()
    rhst_h = nc.dram_tensor("rhst", [D, D], F32, kind="ExternalInput").ap()
    mstd_h = nc.dram_tensor("mstd", [128, 256], BF16, kind="ExternalInput").ap()
    mc0_h = nc.dram_tensor("mc0", [128, 256], BF16, kind="ExternalInput").ap()
    ones_h = nc.dram_tensor("onesc", [128, 1], BF16, kind="ExternalInput").ap()
    ident_h = nc.dram_tensor("ident", [128, 128], F32, kind="ExternalInput").ap()
    sel_h = nc.dram_tensor("sel", [8, 1024], F32, kind="ExternalInput").ap()
    y_h = nc.dram_tensor("y", [2, 2048, D], F32, kind="ExternalOutput").ap()
    w_hs = [wqt_h, wkt_h, wvt_h, wot_h]

    with ExitStack() as es:
        sb = lambda n, shp, dt: es.enter_context(nc.sbuf_tensor(n, shp, dt))
        wbf = sb("wbf", [128, 4 * 8192], BF16)
        qrhs = sb("qrhs", [128, 8192], BF16)
        scr = sb("scr", [128, SCR_COLS], F32)
        q32 = sb("q32t", [128, 1024], I32)
        bfp = sb("bfp", [128, BFP_COLS], BF16)
        bfp2 = sb("bfp2", [128, BFP2_COLS], BF16)
        sm = sb("sm", [128, 64], F32)
        mstd = sb("mstd_s", [128, 256], BF16)
        mc0 = sb("mc0_s", [128, 256], BF16)
        onesc = sb("ones_s", [128, 1], BF16)
        ident = sb("ident_s", [128, 128], F32)
        sel = sb("sel_s", [8, 1024], F32)
        rsrow8 = sb("rsrow8", [8, 128], F32)

        p_proj = es.enter_context(nc.psum_tensor("p_proj", [128, 1024], F32))
        p_sc = es.enter_context(nc.psum_tensor("p_sc", [128, 2048], F32))
        p_av = es.enter_context(nc.psum_tensor("p_av", [128, 1024], F32))

        sem_names = (["ldw", "ldw2", "st", "v", "a", "t", "trs", "lwq", "lwk", "lwv", "lwo"]
                     + [f"{k}{p}" for k in ("trxn", "trql", "trq", "trk", "trao",
                                            "traoa")
                        for p in (0, 1)]
                     + ["lx0", "lx1", "lx2", "lx3"])
        SEMS = {n: es.enter_context(nc.semaphore("s_" + n)) for n in sem_names}
        nc.const_aps.register(F32, MAGIC, sm[:, CMG:CMG + 1])
        nc.const_aps.register(F32, -MAGIC, sm[:, CNMG:CNMG + 1])
        block = es.enter_context(nc.Block())

        CUR = [None]

        def ap3(tensor, cols, off):
            return bass.AP(tensor, off, [[cols, 128], [128, 8], [1, 128]])

        def W(eng, sname, tag):
            if CUR[0].emit:
                eng.wait_ge(SEMS[sname], CUR[0].tag[(sname, tag)])

        def Wv(eng, sname, value):
            if CUR[0].emit:
                eng.wait_ge(SEMS[sname], value)

        def I(instr, sname=None, tag=None, n=1):
            if CUR[0].emit:
                bi = instr()
                if sname is not None:
                    bi.then_inc(SEMS[sname], n)
            if sname is not None:
                CUR[0].inc(sname, n, tag)


        def emit_sync(sync):
            I(lambda: sync.dma_start(mstd[:], mstd_h[:]), "ldw", None, 16)
            I(lambda: sync.dma_start(mc0[:], mc0_h[:]), "ldw", None, 16)
            I(lambda: sync.dma_start(onesc[:], ones_h[:]), "ldw", None, 16)
            I(lambda: sync.dma_start(ident[:], ident_h[:]), "ldw", None, 16)
            I(lambda: sync.dma_start(sel[:], sel_h[:]), "ldw", None, 16)
            I(lambda: sync.dma_start(scr[:, XS[0]:XS[0] + 1024], x_h[0, 0:128, :]),
              "lx0", ("x", 0), 16)
            for i in range(0, 8, 2):
                if i >= 2:
                    W(sync, "v", ("cast", i - 2))
                wst = WST[i % 2]
                src = rhst_h[i * 128:(i + 1) * 128, :]
                I(lambda src=src, wst=wst: sync.dma_start(scr[:, wst:wst + 1024], src),
                  "ldw", ("ldw", i), 16)
            W(sync, "v", ("cast", 7))
            for ntile in range(8):
                out_ap = bass.AP(qrhs, ntile * 128, [[8192, 128], [1024, 8], [1, 128]])
                in_ap = ap3(bfp2, BFP2_COLS, QRHS_T + ntile * 1024)
                I(lambda o=out_ap, ii=in_ap: sync.dma_start_transpose(o, ii),
                  "trs", ("trs", ntile), 16)
            for c in (1, 2):
                if c < nchunk:
                    b1, cc1 = divmod(c, 16)
                    I(lambda b1=b1, cc1=cc1, c=c: sync.dma_start(
                        scr[:, XS[c % 4]:XS[c % 4] + 1024],
                        x_h[b1, cc1 * 128:(cc1 + 1) * 128, :]),
                      f"lx{c % 4}", ("x", c), 16)

            for i in range(-2, nchunk + 2):
                if 3 <= i + 3 < nchunk:
                    c = i + 3
                    if i >= 1:
                        W(sync, "v", ("comb", i - 1))
                    b1, cc1 = divmod(c, 16)
                    I(lambda b1=b1, cc1=cc1, c=c: sync.dma_start(
                        scr[:, XS[c % 4]:XS[c % 4] + 1024],
                        x_h[b1, cc1 * 128:(cc1 + 1) * 128, :]),
                      f"lx{c % 4}", ("x", c), 16)
                if 0 <= i + 1 < nchunk:
                    c = i + 1
                    W(sync, "a", ("qte", c))
                    I(lambda c=c: sync.dma_start_transpose(
                        ap3(bfp, BFP_COLS, QTF), ap3(bfp, BFP_COLS, QROW)),
                      f"trq{c % 2}", ("trq", c), 16)
                if 0 <= i < nchunk:
                    c = i
                    ap4 = lambda off: bass.AP(bfp2, off, [[BFP2_COLS, 128], [128, 4],
                                                          [1, 128]])
                    W(sync, "v", ("ao0", c))
                    I(lambda c=c, ap4=ap4: sync.dma_start_transpose(
                        ap4(AOT[c % 2]), ap4(AO[c % 2])),
                      f"traoa{c % 2}", ("trao0", c), 16)
                    W(sync, "v", ("ao", c))
                    I(lambda c=c, ap4=ap4: sync.dma_start_transpose(
                        ap4(AOT[c % 2] + 512), ap4(AO[c % 2] + 512)),
                      f"trao{c % 2}", ("trao", c), 16)
                if 0 <= i + 1 < nchunk:
                    c = i + 1
                    W(sync, "a", ("kte", c))
                    I(lambda c=c: sync.dma_start_transpose(
                        ap3(bfp, BFP_COLS, KT[c % 2]), ap3(bfp, BFP_COLS, KROW)),
                      f"trk{c % 2}", ("trk", c), 16)
                if 0 <= i + 2 < nchunk:
                    c = i + 2
                    if i >= 0:
                        W(sync, "t", ("ffn", i))
                    W(sync, "a", ("qbf", c))
                    I(lambda c=c: sync.dma_start_transpose(
                        ap3(bfp, BFP_COLS, QLT[c % 2]), ap3(bfp, BFP_COLS, QB[c % 2])),
                      f"trql{c % 2}", ("trql", c), 16)
                    W(sync, "v", ("xn", c))
                    I(lambda c=c: sync.dma_start_transpose(
                        ap3(bfp, BFP_COLS, XNT[c % 2]), ap3(bfp, BFP_COLS, XN[c % 2])),
                      f"trxn{c % 2}", ("trxn", c), 16)
                if 0 <= i < nchunk:
                    c = i
                    b, cc = divmod(c, 16)
                    W(sync, "v", ("comb", c))
                    I(lambda b=b, cc=cc, c=c: sync.dma_start(
                        y_h[b, cc * 128:(cc + 1) * 128, :],
                        scr[:, OUTS[c % 2]:OUTS[c % 2] + 1024]),
                      "st", ("store", c), 16)
            if CUR[0].emit:
                sync.wait_ge(SEMS["st"], 16 * nchunk)

        def emit_vector(vector):
            def SW(instr):
                I(instr, "v", None)
                if CUR[0].emit:
                    vector.wait_ge(SEMS["v"], CUR[0].count["v"])
            I(lambda: vector.memset(bfp[:, QTM[0]:QTM[0] + 2048], 0.0))
            I(lambda: vector.memset(bfp[:, QTM[1]:QTM[1] + 2048], 0.0))
            I(lambda: vector.memset(bfp[:, KT[1]:KT[1] + 1024], 0.0))
            SW(lambda: vector.memset(bfp[:, VS[2]:VS[2] + 1040], 0.0))
            I(lambda: vector.memset(
                bass.AP(bfp, VS[0] + 64, [[BFP_COLS, 128], [65, 16]]), 1.0))
            I(lambda: vector.memset(
                bass.AP(bfp, VS[1] + 64, [[BFP_COLS, 128], [65, 16]]), 1.0))
            I(lambda: vector.memset(
                bass.AP(bfp, VS[2] + 64, [[BFP_COLS, 128], [65, 16]]), 1.0),
              "v", ("rings", 0))

            def LN1(c):
                Wv(vector, f"lx{c % 4}", 16 * (c // 4 + 1))
                SW(lambda c=c: vector.reduce_sum(sm[:, SUMC:SUMC + 1],
                                                 scr[:, XS[c % 4]:XS[c % 4] + 1024], AX.X))
                SW(lambda: vector.tensor_scalar_mul(sm[:, NEGM:NEGM + 1],
                                                    sm[:, SUMC:SUMC + 1], -1.0 / 1024.0))
                if c >= 1:
                    W(vector, "a", ("sq", c - 1))
                SW(lambda c=c: vector.tensor_scalar(scr[:, CEN[c % 2]:CEN[c % 2] + 1024],
                                                    scr[:, XS[c % 4]:XS[c % 4] + 1024],
                                                    sm[:, NEGM:NEGM + 1], None, ALU.add))
                am = AMX2[c % 2]
                SW(lambda c=c, am=am: vector.reduce_max(
                    sm[:, am:am + 1], scr[:, CEN[c % 2]:CEN[c % 2] + 1024], AX.X,
                    apply_absolute_value=True))
                SW(lambda am=am: vector.reciprocal(sm[:, RECA:RECA + 1],
                                                   sm[:, am:am + 1]))
                I(lambda c=c: vector.tensor_scalar_mul(
                    sm[:, QRA[c % 2]:QRA[c % 2] + 1], sm[:, RECA:RECA + 1], 127.0),
                  "v", ("qrd", c))

            def LN2(c):
                rs = RSTD[c % 2]
                sq = SQS[c % 2]
                W(vector, "a", ("sq", c))
                SW(lambda sq=sq: vector.tensor_scalar(sm[:, LNV:LNV + 1],
                                                      sm[:, sq:sq + 1],
                                                      1.0 / 1024.0, 1e-5,
                                                      ALU.mult, ALU.add))
                SW(lambda: vector.tensor_scalar(sm[:, NT1:NT1 + 1], sm[:, LNV:LNV + 1],
                                                RA_, RB_, ALU.mult, ALU.add))
                SW(lambda: vector.tensor_tensor(sm[:, NT2:NT2 + 1], sm[:, NT1:NT1 + 1],
                                                sm[:, LNV:LNV + 1], ALU.mult))
                SW(lambda rs=rs: vector.tensor_scalar(sm[:, rs:rs + 1],
                                                      sm[:, NT2:NT2 + 1],
                                                      RC_, None, ALU.add))
                for _ in range(3):
                    SW(lambda rs=rs: vector.tensor_tensor(
                        sm[:, NT1:NT1 + 1], sm[:, rs:rs + 1], sm[:, rs:rs + 1],
                        ALU.mult))
                    SW(lambda: vector.scalar_tensor_tensor(
                        sm[:, NT2:NT2 + 1], sm[:, NT1:NT1 + 1], -0.5,
                        sm[:, LNV:LNV + 1], ALU.mult, ALU.mult))
                    SW(lambda rs=rs: vector.scalar_tensor_tensor(
                        sm[:, rs:rs + 1], sm[:, NT2:NT2 + 1], 1.5, sm[:, rs:rs + 1],
                        ALU.add, ALU.mult))
                SW(lambda c=c, rs=rs: vector.scalar_tensor_tensor(
                    sm[:, LS[c % 2]:LS[c % 2] + 1], sm[:, AMX2[c % 2]:AMX2[c % 2] + 1],
                    1.0 / 127.0, sm[:, rs:rs + 1], ALU.mult, ALU.mult))
                I(lambda c=c, rs=rs: vector.tensor_scalar(
                    bfp[:, XN[c % 2]:XN[c % 2] + 1024],
                    scr[:, CEN[c % 2]:CEN[c % 2] + 1024],
                    sm[:, rs:rs + 1], None, ALU.mult), "v", ("xn", c))

            I(lambda: vector.memset(sm[:, CMG:CMG + 1], MAGIC))
            I(lambda: vector.memset(sm[:, CNMG:CNMG + 1], -MAGIC))
            LN1(0)
            for i in range(8):
                if i % 2 == 0:
                    Wv(vector, "ldw", 80 + 16 * (i // 2 + 1))
                else:
                    Wv(vector, "ldw2", 16 * (i // 2 + 1))
                wst = WST[i % 2]
                stage = scr[:, wst:wst + 1024]
                SW(lambda s=stage: vector.reduce_max(sm[:, AMAX:AMAX + 1], s, AX.X,
                                                     apply_absolute_value=True))
                I(lambda i=i: vector.tensor_scalar_mul(
                    sm[:, RSTL + i:RSTL + i + 1], sm[:, AMAX:AMAX + 1], 1.0 / 127.0))
                SW(lambda: vector.reciprocal(sm[:, RECA:RECA + 1], sm[:, AMAX:AMAX + 1]))
                SW(lambda s=stage: vector.tensor_scalar(
                    scr[:, T1_:T1_ + 1024], s, sm[:, RECA:RECA + 1], 127.0,
                    ALU.mult, ALU.mult))
                SW(lambda: vector.tensor_scalar(q32[:], scr[:, T1_:T1_ + 1024],
                                                MAGIC, -MAGIC, ALU.add, ALU.add))
                I(lambda i=i: vector.tensor_copy(
                    bfp2[:, QRHS_T + i * 1024:QRHS_T + (i + 1) * 1024], q32[:]),
                  "v", ("cast", i))
            W(vector, "t", ("rst", 0))
            I(lambda: vector.tensor_copy(rsrow8[:], p_av[0:8, 0:128]), "v", ("rsrow", 0))
            W(vector, "t", ("rsb", 0))
            I(lambda: vector.tensor_copy(scr[:, RSB:RSB + 1024], p_proj[:, 0:1024]),
              "v", ("rsbe", 0))
            LN2(0)
            LN1(1)

            for i in range(-2, nchunk + 2):
                if 0 <= i < nchunk:
                    c = i
                    mask = mc0 if c % 16 == 0 else mstd
                    for g in range(2):
                        W(vector, "a", ("e0" if g == 0 else "e1", c))
                        erv = bass.AP(bfp2, ERAW + g * 2048,
                                      [[BFP2_COLS, 128], [256, 8], [1, 256]])
                        ebv = bass.AP(bfp2, EBF + g * 2048,
                                      [[BFP2_COLS, 128], [256, 8], [1, 256]])
                        mkv = bass.AP(mask, 0, [[256, 128], [0, 8], [1, 256]])
                        I(lambda e=ebv, r=erv, m=mkv: vector.tensor_tensor(e, r, m, ALU.mult),
                          "v", ("e0m" if g == 0 else "e1m", c))
                if 0 <= i < nchunk:
                    c = i
                    W(vector, "t", ("ava", c))
                    SW(lambda: vector.reciprocal(
                        sm[:, RSUM:RSUM + 8],
                        bass.AP(p_sc, 64, [[2048, 128], [128, 8]])))
                    aov = bass.AP(bfp2, AO[c % 2], [[BFP2_COLS, 128], [64, 8], [1, 64]])
                    pavv = bass.AP(p_sc, 0, [[2048, 128], [128, 8], [1, 64]])
                    rsv = bass.AP(sm, RSUM, [[64, 128], [1, 8], [0, 64]])
                    I(lambda a=aov, p=pavv, r=rsv: vector.tensor_tensor(a, p, r, ALU.mult),
                      "v", ("ao0", c))
                    W(vector, "t", ("av", c))
                    SW(lambda: vector.reciprocal(
                        sm[:, RSUM + 8:RSUM + 16],
                        bass.AP(p_sc, 1024 + 64, [[2048, 128], [128, 8]])))
                    aov = bass.AP(bfp2, AO[c % 2] + 512,
                                  [[BFP2_COLS, 128], [64, 8], [1, 64]])
                    pavv = bass.AP(p_sc, 1024, [[2048, 128], [128, 8], [1, 64]])
                    rsv = bass.AP(sm, RSUM + 8, [[64, 128], [1, 8], [0, 64]])
                    I(lambda a=aov, p=pavv, r=rsv: vector.tensor_tensor(a, p, r, ALU.mult),
                      "v", ("ao", c))
                if 0 <= i + 2 < nchunk and i >= -1:
                    LN2(i + 2)
                if 0 <= i + 1 < nchunk:
                    c = i + 1
                    W(vector, "t", ("ffn", c))
                    if c >= 1:
                        W(vector, "a", ("silu", c - 1))
                    I(lambda c=c: vector.scalar_tensor_tensor(
                        scr[:, GPRE:GPRE + 512], p_proj[:, 0:512],
                        sm[:, LS[c % 2]:LS[c % 2] + 1], scr[:, RSB:RSB + 512],
                        ALU.mult, ALU.mult), "v", ("ffne0", c))
                    I(lambda c=c: vector.scalar_tensor_tensor(
                        scr[:, GPRE + 512:GPRE + 1024], p_proj[:, 512:1024],
                        sm[:, LS[c % 2]:LS[c % 2] + 1], scr[:, RSB + 512:RSB + 1024],
                        ALU.mult, ALU.mult), "v", ("ffne", c))
                if 0 <= i < nchunk:
                    c = i
                    W(vector, "t", ("op", c))
                    if c >= 2:
                        Wv(vector, "st", 16 * c)
                    SW(lambda c=c: vector.tensor_tensor(scr[:, T1_:T1_ + 1024],
                                                        p_av[:, 0:1024],
                                                        scr[:, GS[c % 2]:GS[c % 2] + 1024],
                                                        ALU.mult))
                    I(lambda c=c: vector.scalar_tensor_tensor(
                        scr[:, OUTS[c % 2]:OUTS[c % 2] + 1024], scr[:, T1_:T1_ + 1024],
                        0.5, scr[:, XS[c % 4]:XS[c % 4] + 1024], ALU.mult, ALU.add),
                      "v", ("comb", c))
                if 0 <= i + 1 < nchunk:
                    c = i + 1
                    W(vector, "a", ("silu", c))
                    I(lambda c=c: vector.scalar_tensor_tensor(
                        scr[:, GS[c % 2]:GS[c % 2] + 1024],
                        scr[:, GS[c % 2]:GS[c % 2] + 1024], 1.0,
                        scr[:, GPRE:GPRE + 1024], ALU.add, ALU.mult), "v", ("g2", c))
                if 0 <= i + 3 < nchunk and i >= -1:
                    LN1(i + 3)

        def emit_scalar(scalar):
            def SWS(instr, sname=None, tag=None):
                I(instr, sname or "a", tag)
                if CUR[0].emit:
                    scalar.wait_ge(SEMS[sname or "a"], CUR[0].count[sname or "a"])
            I(lambda: scalar.preload_act_table(AF.Exp))
            for ti in range(1, 8, 2):
                if ti >= 2:
                    W(scalar, "v", ("cast", ti - 2))
                wst = WST[ti % 2]
                src = rhst_h[ti * 128:(ti + 1) * 128, :]
                I(lambda src=src, wst=wst: scalar.dma_start(
                    scr[:, wst:wst + 1024], src), "ldw2", ("ldw2", ti), 16)
            for i in range(-2, nchunk + 2):
                if 0 <= i < nchunk:
                    c = i
                    W(scalar, "t", ("sc0", c))
                    if c >= 1:
                        W(scalar, "v", ("e0m", c - 1))
                    I(lambda: scalar.activation(bfp2[:, ERAW:ERAW + 2048],
                                                p_sc[:, 0:2048], AF.Exp, scale=0.125),
                      "a", ("e0", c))
                if 0 <= i + 1 < nchunk:
                    c = i + 1
                    W(scalar, "t", ("q", c))
                    I(lambda: scalar.copy(bfp[:, QROW:QROW + 1024], p_proj[:, 0:1024]),
                      "a", ("qte", c))
                if 0 <= i < nchunk:
                    c = i
                    W(scalar, "t", ("sc1", c))
                    if c >= 1:
                        W(scalar, "v", ("e1m", c - 1))
                    I(lambda: scalar.activation(bfp2[:, ERAW + 2048:ERAW + 3072],
                                                p_sc[:, 0:1024], AF.Exp, scale=0.125),
                      "a", ("e1h", c))
                    I(lambda: scalar.activation(bfp2[:, ERAW + 3072:ERAW + 4096],
                                                p_sc[:, 1024:2048], AF.Exp, scale=0.125),
                      "a", ("e1", c))
                if 0 <= i + 1 < nchunk:
                    c = i + 1
                    W(scalar, "t", ("v", c))
                    I(lambda c=c: scalar.copy(
                        bass.AP(bfp, VS[c % 3], [[BFP_COLS, 128], [65, 8], [1, 64]]),
                        bass.AP(p_proj, 0, [[1024, 128], [64, 8], [1, 64]])),
                      "a", ("vev0", c))
                    I(lambda c=c: scalar.copy(
                        bass.AP(bfp, VS[c % 3] + 8 * 65,
                                [[BFP_COLS, 128], [65, 8], [1, 64]]),
                        bass.AP(p_proj, 512, [[1024, 128], [64, 8], [1, 64]])),
                      "a", ("vev", c))
                if 0 <= i + 2 < nchunk:
                    c = i + 2
                    W(scalar, "v", ("qrd", c))
                    if c == 0:
                        W(scalar, "v", ("cast", 7))
                    if c >= 2:
                        Wv(scalar, f"trql{c % 2}", 16 * ((c - 2) // 2 + 1))
                    SWS(lambda c=c: scalar.activation(
                        q32[:], scr[:, CEN[c % 2]:CEN[c % 2] + 1024], AF.Copy,
                        scale=sm[:, QRA[c % 2]:QRA[c % 2] + 1], bias=MAGIC))
                    SWS(lambda c=c: scalar.activation(
                        bfp[:, QB[c % 2]:QB[c % 2] + 1024], q32[:], AF.Copy,
                        bias=-MAGIC), "a", ("qbf", c))
                if 0 <= i + 2 < nchunk:
                    c = i + 2
                    I(lambda c=c: scalar.activation(
                        scr[:, CEN[(c + 1) % 2]:CEN[(c + 1) % 2] + 1024],
                        scr[:, CEN[c % 2]:CEN[c % 2] + 1024], AF.Square,
                        accum_out=sm[:, SQS[c % 2]:SQS[c % 2] + 1]), "a", ("sq", c))
                if 0 <= i + 1 < nchunk:
                    c = i + 1
                    W(scalar, "t", ("k", c))
                    I(lambda: scalar.copy(bfp[:, KROW:KROW + 512], p_proj[:, 0:512]),
                      "a", ("kte0", c))
                    I(lambda: scalar.copy(bfp[:, KROW + 512:KROW + 1024],
                                          p_proj[:, 512:1024]), "a", ("kte", c))
                if 0 <= i + 1 < nchunk:
                    c = i + 1
                    Wv(scalar, f"trq{c % 2}", 16 * (c // 2 + 1))
                    I(lambda c=c: scalar.copy(bfp[0:64, QTM[c % 2]:QTM[c % 2] + 1024],
                                              bfp[0:64, QTF:QTF + 1024]))
                    I(lambda c=c: scalar.copy(
                        bfp[64:128, QTM[c % 2] + 1024:QTM[c % 2] + 2048],
                        bfp[64:128, QTF:QTF + 1024]), "a", ("qtm", c))
                if 0 <= i + 1 < nchunk:
                    c = i + 1
                    W(scalar, "v", ("ffne", c))
                    I(lambda c=c: scalar.activation(scr[:, GS[c % 2]:GS[c % 2] + 1024],
                                                    scr[:, GPRE:GPRE + 1024], AF.Tanh,
                                                    scale=0.5), "a", ("silu", c))

        def emit_tensor(tensor):
            W(tensor, "v", ("cast", 7))
            I(lambda: tensor.transpose(p_av[0:8, 0:128], sm[:, RSTL:RSTL + 8], ident[:]),
              "t", ("rst", 0))
            W(tensor, "v", ("rsrow", 0))

            def IMM(mms, sname, tag):
                if CUR[0].emit:
                    last = None
                    for f in mms:
                        r = f()
                        last = r if r is not None else last
                    last.then_inc(SEMS[sname], 1)
                CUR[0].inc(sname, 1, (tag[0], tag[1]))

            IMM([(lambda t=t: tensor.matmul(p_proj[:, t * 128:(t + 1) * 128],
                                            sel[:, t * 128:(t + 1) * 128], rsrow8[:],
                                            start=True, stop=True)) for t in range(8)],
                "t", ("rsb", 0))
            Wv(tensor, "lx0", 16)
            IMM([(lambda r=r: tensor.matmul(p_sc[:, (r % 4) * 512:(r % 4) * 512 + 512],
                                            ident[:], scr[:, XS[0]:XS[0] + 512],
                                            start=True, stop=True))
                 for r in range(48)], "t", ("warm", 0))

            def PROJ(c, name, lhs_t, lhs_base, rhs_t, rhs_base, half1_wait=None):
                mms = []
                for kc in range(8):
                    for half in range(2):
                        if kc == 0 and half == 1 and half1_wait is not None:
                            mms.append(half1_wait)
                        mms.append(lambda kc=kc, half=half: tensor.matmul(
                            p_proj[:, half * 512:(half + 1) * 512],
                            lhs_t[:, lhs_base + kc * 128:lhs_base + kc * 128 + 128],
                            rhs_t[:, rhs_base + kc * 1024 + half * 512:
                                  rhs_base + kc * 1024 + half * 512 + 512],
                            start=(kc == 0), stop=(kc == 7)))
                IMM(mms, "t", (name, c))

            def SCORES(c, g):
                qtm = QTM[c % 2]
                ktc = KT[c % 2]
                ktp = KT[(c + 1) % 2]
                mms = []
                for hh in range(8):
                    h = g * 8 + hh
                    qcol = qtm + (h % 2) * 1024 + (h // 2) * 128
                    tcol = (h // 2) * 128
                    mms.append(lambda hh=hh, qcol=qcol, tcol=tcol: tensor.matmul(
                        p_sc[:, hh * 256:hh * 256 + 128],
                        bfp[:, ktp + tcol:ktp + tcol + 128],
                        bfp[:, qcol:qcol + 128],
                        start=True, stop=True))
                    mms.append(lambda hh=hh, qcol=qcol, tcol=tcol: tensor.matmul(
                        p_sc[:, hh * 256 + 128:hh * 256 + 256],
                        bfp[:, ktc + tcol:ktc + tcol + 128],
                        bfp[:, qcol:qcol + 128],
                        start=True, stop=True))
                IMM(mms, "t", ("sc0" if g == 0 else "sc1", c))

            def AVS(c, grp):
                vc_ = VS[c % 3]
                vp_ = VS[(c + 2) % 3]
                mms = []
                for h in range(grp * 8, grp * 8 + 8):
                    g, hh = divmod(h, 8)
                    e0 = EBF + g * 2048 + hh * 256
                    mms.append(lambda h=h, e0=e0: tensor.matmul(
                        p_sc[:, h * 128:h * 128 + 65], bfp2[:, e0:e0 + 128],
                        bfp[:, vp_ + h * 65:vp_ + h * 65 + 65], start=True, stop=False))
                    mms.append(lambda h=h, e0=e0: tensor.matmul(
                        p_sc[:, h * 128:h * 128 + 65], bfp2[:, e0 + 128:e0 + 256],
                        bfp[:, vc_ + h * 65:vc_ + h * 65 + 65], start=False, stop=True))
                IMM(mms, "t", ("ava" if grp == 0 else "av", c))

            for i in range(-2, nchunk + 2):
                if 0 <= i < nchunk:
                    W(tensor, "a", ("qtm", i))
                    Wv(tensor, f"trk{i % 2}", 16 * (i // 2 + 1))
                    if i >= 1:
                        W(tensor, "v", ("ao", i - 1))
                    SCORES(i, 0)
                if 0 <= i + 1 < nchunk:
                    c = i + 1
                    Wv(tensor, f"trxn{c % 2}", 16 * (c // 2 + 1))
                    if c == 0:
                        Wv(tensor, "lwq", 128)
                    if i >= 0:
                        W(tensor, "v", ("ffne0", i))
                        PROJ(c, "q", bfp, XNT[c % 2], wbf, 0 * 8192,
                             half1_wait=lambda i=i: W(tensor, "v", ("ffne", i)))
                    else:
                        W(tensor, "v", ("rsbe", 0))
                        PROJ(c, "q", bfp, XNT[c % 2], wbf, 0 * 8192)
                if 0 <= i < nchunk:
                    W(tensor, "a", ("e0", i))
                    SCORES(i, 1)
                if 0 <= i + 1 < nchunk:
                    c = i + 1
                    W(tensor, "a", ("qte", c))
                    if c == 0:
                        Wv(tensor, "lwv", 128)
                    PROJ(c, "v", bfp, XNT[c % 2], wbf, 2 * 8192)
                if 0 <= i < nchunk:
                    c = i
                    W(tensor, "a", ("e1h", c))
                    W(tensor, "v", ("e0m", c))
                    AVS(c, 0)
                    W(tensor, "a", ("e1", c))
                    W(tensor, "v", ("e1m", c))
                    AVS(c, 1)
                if 0 <= i + 1 < nchunk:
                    c = i + 1
                    W(tensor, "a", ("vev0", c))
                    if c == 0:
                        Wv(tensor, "lwk", 128)
                    PROJ(c, "k", bfp, XNT[c % 2], wbf, 1 * 8192,
                         half1_wait=lambda c=c: W(tensor, "a", ("vev", c)))
                if 0 <= i < nchunk:
                    c = i
                    Wv(tensor, f"traoa{c % 2}", 16 * (c // 2 + 1))
                    if c >= 1:
                        W(tensor, "v", ("comb", c - 1))
                    if c == 0:
                        Wv(tensor, "lwo", 128)
                    mms = []
                    for kc in range(8):
                        if kc == 4:
                            mms.append(lambda c=c: Wv(
                                tensor, f"trao{c % 2}", 16 * (c // 2 + 1)))
                        for half in range(2):
                            mms.append(lambda kc=kc, half=half, c=c: tensor.matmul(
                                p_av[:, half * 512:(half + 1) * 512],
                                bfp2[:, AOT[c % 2] + kc * 128:AOT[c % 2] + kc * 128 + 128],
                                wbf[:, 3 * 8192 + kc * 1024 + half * 512:
                                    3 * 8192 + kc * 1024 + half * 512 + 512],
                                start=(kc == 0), stop=(kc == 7)))
                    IMM(mms, "t", ("op", c))
                if 0 <= i + 1 < nchunk:
                    c = i + 1
                    W(tensor, "a", ("kte0", c))
                    Wv(tensor, f"trql{c % 2}", 16 * (c // 2 + 1))
                    if c == 0:
                        Wv(tensor, "trs", 128)
                    PROJ(c, "ffn", bfp, QLT[c % 2], qrhs, 0,
                         half1_wait=lambda c=c: W(tensor, "a", ("kte", c)))

        def emit_gpsimd(gpsimd):
            for wi, (wh, semn) in enumerate(zip(w_hs, ("lwq", "lwk", "lwv", "lwo"))):
                for kc in range(8):
                    I(lambda wi=wi, wh=wh, kc=kc: gpsimd.dma_start(
                        wbf[:, wi * 8192 + kc * 1024:wi * 8192 + kc * 1024 + 1024],
                        wh[kc * 128:(kc + 1) * 128, :]), semn, ("lw", wi, kc), 16)

        L1 = Ledger()
        CUR[0] = L1

        class FakeEng:
            def __getattr__(self, name):
                raise RuntimeError("pass-1 must not touch the engine")
        fe = FakeEng()
        emit_sync(fe)
        emit_vector(fe)
        emit_scalar(fe)
        emit_tensor(fe)
        emit_gpsimd(fe)

        L2 = Ledger()
        L2.tag = dict(L1.tag)
        L2.emit = True
        CUR[0] = L2

        @block.sync
        def _(sync):
            emit_sync(sync)

        @block.vector
        def _(vector):
            emit_vector(vector)

        @block.scalar
        def _(scalar):
            emit_scalar(scalar)

        @block.tensor
        def _(tensor):
            emit_tensor(tensor)

        @block.gpsimd
        def _(gpsimd):
            emit_gpsimd(gpsimd)

    return nc


_NC_CACHE = [None]


def kernel(L0_attn_x, L0_attn_Wq, L0_attn_Wk, L0_attn_Wv, L0_attn_Wo, L0_ffn_rhs):
    x = np.ascontiguousarray(np.asarray(L0_attn_x, dtype=np.float32))
    wqt = np.ascontiguousarray(np.asarray(L0_attn_Wq, dtype=np.float32).T)
    wkt = np.ascontiguousarray(np.asarray(L0_attn_Wk, dtype=np.float32).T)
    wvt = np.ascontiguousarray(np.asarray(L0_attn_Wv, dtype=np.float32).T)
    wot = np.ascontiguousarray(np.asarray(L0_attn_Wo, dtype=np.float32).T)
    rhst = np.ascontiguousarray(np.asarray(L0_ffn_rhs, dtype=np.float32).T)

    bf = ml_dtypes.bfloat16
    jj = np.arange(128)[:, None]
    ii = np.arange(128)[None, :]
    m_prev = (jj > ii).astype(np.float32)
    m_cur = (jj <= ii).astype(np.float32)
    mstd = np.concatenate([m_prev, m_cur], axis=1).astype(bf)
    mc0 = np.concatenate([np.zeros((128, 128), np.float32), m_cur], axis=1).astype(bf)
    onesc = np.ones((128, 1), np.float32).astype(bf)
    ident = np.eye(128, dtype=np.float32)
    sel = np.zeros((8, 1024), np.float32)
    for t in range(8):
        sel[t, t * 128:(t + 1) * 128] = 1.0

    if _NC_CACHE[0] is None:
        _NC_CACHE[0] = build()
    nc = _NC_CACHE[0]

    in_maps = []
    for i in range(8):
        in_maps.append({
            "x": x[2 * i:2 * i + 2],
            "wqt": wqt, "wkt": wkt, "wvt": wvt, "wot": wot, "rhst": rhst,
            "mstd": mstd, "mc0": mc0, "onesc": onesc, "ident": ident, "sel": sel,
        })
    res = run_bass_kernel_spmd(nc, in_maps, core_ids=list(range(8)))
    out = np.concatenate([np.asarray(r["y"]) for r in res.results], axis=0)
    return out.reshape(16, 2048, 1024).astype(np.float32)


# version 26
# speedup vs baseline: 1.0385x; 1.0385x over previous
import numpy as np
import ml_dtypes
import concourse.bass as bass
import concourse.mybir as mybir
from concourse.bass_utils import run_bass_kernel_spmd
from contextlib import ExitStack

F32 = mybir.dt.float32
BF16 = mybir.dt.bfloat16
I32 = mybir.dt.int32
AF = mybir.ActivationFunctionType
ALU = mybir.AluOpType
AX = mybir.AxisListType

MAGIC = 12582912.0
RA_, RB_, RC_ = 0.26926346465863016, -1.1080260091717, 1.8587355389354314
NCHUNK = 32
D = 1024
NH = 16

XS = [0, 1024, 2048, 3072]
CEN = [4096, 5120]
T1_ = 6144
GPRE = 7168
GS = [8192, 9216]
OUTS = [10240, 11264]
RSB = 12288
SCR_COLS = 13312
WST = OUTS

XN = [0, 1024]
XNT = [2048, 3072]
QB = [4096, 5120]
QLT = [6144, 7168]
KT = [8192, 9216]
VS = [10240, 11280, 12320]
QTF = 13360
QTM = [14384, 16432]
QROW, KROW = 18480, 19504
BFP_COLS = 20560

ERAW = 0
EBF = 4096
AO = [8192, 9216]
AOT = [10240, 11264]
BFP2_COLS = 12288
QRHS_T = 0

SUMC, NEGM, AMAX, RECA, SQS0, LNV, NT1, NT2 = range(8)
RSTD = [8, 9]
LS = [10, 11]
SQS = [SQS0, 12]
QRA = [13, 14]
CMG, CNMG = 32, 33
AMX2 = [34, 35]
RSUM = 16
RSTL = 40


class Ledger:
    def __init__(self):
        self.count = {}
        self.tag = {}
        self.emit = False

    def inc(self, sem_name, n, tag=None):
        c = self.count.get(sem_name, 0) + n
        self.count[sem_name] = c
        if tag is not None:
            self.tag[(sem_name, tag)] = c
        return c


def build(nchunk=NCHUNK):
    nc = bass.Bass("TRN2", target_bir_lowering=False)
    x_h = nc.dram_tensor("x", [2, 2048, D], F32, kind="ExternalInput").ap()
    wqt_h = nc.dram_tensor("wqt", [D, D], F32, kind="ExternalInput").ap()
    wkt_h = nc.dram_tensor("wkt", [D, D], F32, kind="ExternalInput").ap()
    wvt_h = nc.dram_tensor("wvt", [D, D], F32, kind="ExternalInput").ap()
    wot_h = nc.dram_tensor("wot", [D, D], F32, kind="ExternalInput").ap()
    rhst_h = nc.dram_tensor("rhst", [D, D], F32, kind="ExternalInput").ap()
    mstd_h = nc.dram_tensor("mstd", [128, 256], BF16, kind="ExternalInput").ap()
    mc0_h = nc.dram_tensor("mc0", [128, 256], BF16, kind="ExternalInput").ap()
    ones_h = nc.dram_tensor("onesc", [128, 1], BF16, kind="ExternalInput").ap()
    ident_h = nc.dram_tensor("ident", [128, 128], F32, kind="ExternalInput").ap()
    sel_h = nc.dram_tensor("sel", [8, 1024], F32, kind="ExternalInput").ap()
    y_h = nc.dram_tensor("y", [2, 2048, D], F32, kind="ExternalOutput").ap()
    w_hs = [wqt_h, wkt_h, wvt_h, wot_h]

    with ExitStack() as es:
        sb = lambda n, shp, dt: es.enter_context(nc.sbuf_tensor(n, shp, dt))
        wbf = sb("wbf", [128, 4 * 8192], BF16)
        qrhs = sb("qrhs", [128, 8192], BF16)
        scr = sb("scr", [128, SCR_COLS], F32)
        q32 = sb("q32t", [128, 1024], I32)
        bfp = sb("bfp", [128, BFP_COLS], BF16)
        bfp2 = sb("bfp2", [128, BFP2_COLS], BF16)
        sm = sb("sm", [128, 64], F32)
        mstd = sb("mstd_s", [128, 256], BF16)
        mc0 = sb("mc0_s", [128, 256], BF16)
        onesc = sb("ones_s", [128, 1], BF16)
        ident = sb("ident_s", [128, 128], F32)
        sel = sb("sel_s", [8, 1024], F32)
        rsrow8 = sb("rsrow8", [8, 128], F32)

        p_proj = es.enter_context(nc.psum_tensor("p_proj", [128, 1024], F32))
        p_sc = es.enter_context(nc.psum_tensor("p_sc", [128, 2048], F32))
        p_av = es.enter_context(nc.psum_tensor("p_av", [128, 1024], F32))

        sem_names = (["ldw", "ldw2", "st", "v", "a", "t", "trs", "lwq", "lwk", "lwv", "lwo"]
                     + [f"{k}{p}" for k in ("trxn", "trql", "trq", "trk", "trao",
                                            "traoa")
                        for p in (0, 1)]
                     + ["lx0", "lx1", "lx2", "lx3"])
        SEMS = {n: es.enter_context(nc.semaphore("s_" + n)) for n in sem_names}
        nc.const_aps.register(F32, MAGIC, sm[:, CMG:CMG + 1])
        nc.const_aps.register(F32, -MAGIC, sm[:, CNMG:CNMG + 1])
        block = es.enter_context(nc.Block())

        CUR = [None]

        def ap3(tensor, cols, off):
            return bass.AP(tensor, off, [[cols, 128], [128, 8], [1, 128]])

        def W(eng, sname, tag):
            if CUR[0].emit:
                eng.wait_ge(SEMS[sname], CUR[0].tag[(sname, tag)])

        def Wv(eng, sname, value):
            if CUR[0].emit:
                eng.wait_ge(SEMS[sname], value)

        def I(instr, sname=None, tag=None, n=1):
            if CUR[0].emit:
                bi = instr()
                if sname is not None:
                    bi.then_inc(SEMS[sname], n)
            if sname is not None:
                CUR[0].inc(sname, n, tag)


        def emit_sync(sync):
            I(lambda: sync.dma_start(mstd[:], mstd_h[:]), "ldw", None, 16)
            I(lambda: sync.dma_start(mc0[:], mc0_h[:]), "ldw", None, 16)
            I(lambda: sync.dma_start(onesc[:], ones_h[:]), "ldw", None, 16)
            I(lambda: sync.dma_start(ident[:], ident_h[:]), "ldw", None, 16)
            I(lambda: sync.dma_start(sel[:], sel_h[:]), "ldw", None, 16)
            I(lambda: sync.dma_start(scr[:, XS[0]:XS[0] + 1024], x_h[0, 0:128, :]),
              "lx0", ("x", 0), 16)
            for i in range(0, 8, 2):
                if i >= 2:
                    W(sync, "v", ("cast", i - 2))
                wst = WST[i % 2]
                src = rhst_h[i * 128:(i + 1) * 128, :]
                I(lambda src=src, wst=wst: sync.dma_start(scr[:, wst:wst + 1024], src),
                  "ldw", ("ldw", i), 16)
            W(sync, "v", ("cast", 7))
            for ntile in range(8):
                out_ap = bass.AP(qrhs, ntile * 128, [[8192, 128], [1024, 8], [1, 128]])
                in_ap = ap3(bfp2, BFP2_COLS, QRHS_T + ntile * 1024)
                I(lambda o=out_ap, ii=in_ap: sync.dma_start_transpose(o, ii),
                  "trs", ("trs", ntile), 16)
            for c in (1, 2):
                if c < nchunk:
                    b1, cc1 = divmod(c, 16)
                    I(lambda b1=b1, cc1=cc1, c=c: sync.dma_start(
                        scr[:, XS[c % 4]:XS[c % 4] + 1024],
                        x_h[b1, cc1 * 128:(cc1 + 1) * 128, :]),
                      f"lx{c % 4}", ("x", c), 16)

            for i in range(-2, nchunk + 2):
                if 3 <= i + 3 < nchunk:
                    c = i + 3
                    if i >= 1:
                        W(sync, "v", ("comb", i - 1))
                    b1, cc1 = divmod(c, 16)
                    I(lambda b1=b1, cc1=cc1, c=c: sync.dma_start(
                        scr[:, XS[c % 4]:XS[c % 4] + 1024],
                        x_h[b1, cc1 * 128:(cc1 + 1) * 128, :]),
                      f"lx{c % 4}", ("x", c), 16)
                if 0 <= i + 1 < nchunk:
                    c = i + 1
                    W(sync, "a", ("qte", c))
                    I(lambda c=c: sync.dma_start_transpose(
                        ap3(bfp, BFP_COLS, QTF), ap3(bfp, BFP_COLS, QROW)),
                      f"trq{c % 2}", ("trq", c), 16)
                if 0 <= i < nchunk:
                    c = i
                    ap4 = lambda off: bass.AP(bfp2, off, [[BFP2_COLS, 128], [128, 4],
                                                          [1, 128]])
                    W(sync, "v", ("ao0", c))
                    I(lambda c=c, ap4=ap4: sync.dma_start_transpose(
                        ap4(AOT[c % 2]), ap4(AO[c % 2])),
                      f"traoa{c % 2}", ("trao0", c), 16)
                    W(sync, "v", ("ao", c))
                    I(lambda c=c, ap4=ap4: sync.dma_start_transpose(
                        ap4(AOT[c % 2] + 512), ap4(AO[c % 2] + 512)),
                      f"trao{c % 2}", ("trao", c), 16)
                if 0 <= i + 1 < nchunk:
                    c = i + 1
                    W(sync, "a", ("kte", c))
                    I(lambda c=c: sync.dma_start_transpose(
                        ap3(bfp, BFP_COLS, KT[c % 2]), ap3(bfp, BFP_COLS, KROW)),
                      f"trk{c % 2}", ("trk", c), 16)
                if 0 <= i + 2 < nchunk:
                    c = i + 2
                    if i >= 0:
                        W(sync, "t", ("ffn", i))
                    W(sync, "a", ("qbf", c))
                    I(lambda c=c: sync.dma_start_transpose(
                        ap3(bfp, BFP_COLS, QLT[c % 2]), ap3(bfp, BFP_COLS, QB[c % 2])),
                      f"trql{c % 2}", ("trql", c), 16)
                    W(sync, "v", ("xn", c))
                    I(lambda c=c: sync.dma_start_transpose(
                        ap3(bfp, BFP_COLS, XNT[c % 2]), ap3(bfp, BFP_COLS, XN[c % 2])),
                      f"trxn{c % 2}", ("trxn", c), 16)
                if 0 <= i < nchunk:
                    c = i
                    b, cc = divmod(c, 16)
                    W(sync, "v", ("comb", c))
                    I(lambda b=b, cc=cc, c=c: sync.dma_start(
                        y_h[b, cc * 128:(cc + 1) * 128, :],
                        scr[:, OUTS[c % 2]:OUTS[c % 2] + 1024]),
                      "st", ("store", c), 16)
            if CUR[0].emit:
                sync.wait_ge(SEMS["st"], 16 * nchunk)

        def emit_vector(vector):
            def SW(instr):
                I(instr, "v", None)
                if CUR[0].emit:
                    vector.wait_ge(SEMS["v"], CUR[0].count["v"])
            I(lambda: vector.memset(bfp[:, QTM[0]:QTM[0] + 2048], 0.0))
            I(lambda: vector.memset(bfp[:, QTM[1]:QTM[1] + 2048], 0.0))
            I(lambda: vector.memset(bfp[:, KT[1]:KT[1] + 1024], 0.0))
            SW(lambda: vector.memset(bfp[:, VS[2]:VS[2] + 1040], 0.0))
            I(lambda: vector.memset(
                bass.AP(bfp, VS[0] + 64, [[BFP_COLS, 128], [65, 16]]), 1.0))
            I(lambda: vector.memset(
                bass.AP(bfp, VS[1] + 64, [[BFP_COLS, 128], [65, 16]]), 1.0))
            I(lambda: vector.memset(
                bass.AP(bfp, VS[2] + 64, [[BFP_COLS, 128], [65, 16]]), 1.0),
              "v", ("rings", 0))

            def LN1A(c):
                Wv(vector, f"lx{c % 4}", 16 * (c // 4 + 1))
                SW(lambda c=c: vector.reduce_sum(sm[:, SUMC:SUMC + 1],
                                                 scr[:, XS[c % 4]:XS[c % 4] + 1024], AX.X))
                SW(lambda: vector.tensor_scalar_mul(sm[:, NEGM:NEGM + 1],
                                                    sm[:, SUMC:SUMC + 1], -1.0 / 1024.0))
                if c >= 1:
                    W(vector, "a", ("sq", c - 1))
                SW(lambda c=c: vector.tensor_scalar(scr[:, CEN[c % 2]:CEN[c % 2] + 1024],
                                                    scr[:, XS[c % 4]:XS[c % 4] + 1024],
                                                    sm[:, NEGM:NEGM + 1], None, ALU.add))

            def LN1B(c):
                am = AMX2[c % 2]
                SW(lambda c=c, am=am: vector.reduce_max(
                    sm[:, am:am + 1], scr[:, CEN[c % 2]:CEN[c % 2] + 1024], AX.X,
                    apply_absolute_value=True))
                SW(lambda am=am: vector.reciprocal(sm[:, RECA:RECA + 1],
                                                   sm[:, am:am + 1]))
                I(lambda c=c: vector.tensor_scalar_mul(
                    sm[:, QRA[c % 2]:QRA[c % 2] + 1], sm[:, RECA:RECA + 1], 127.0),
                  "v", ("qrd", c))

            def LN1(c):
                LN1A(c)
                LN1B(c)

            def LN2(c):
                rs = RSTD[c % 2]
                sq = SQS[c % 2]
                W(vector, "a", ("sq", c))
                SW(lambda sq=sq: vector.tensor_scalar(sm[:, LNV:LNV + 1],
                                                      sm[:, sq:sq + 1],
                                                      1.0 / 1024.0, 1e-5,
                                                      ALU.mult, ALU.add))
                SW(lambda: vector.tensor_scalar(sm[:, NT1:NT1 + 1], sm[:, LNV:LNV + 1],
                                                RA_, RB_, ALU.mult, ALU.add))
                SW(lambda: vector.tensor_tensor(sm[:, NT2:NT2 + 1], sm[:, NT1:NT1 + 1],
                                                sm[:, LNV:LNV + 1], ALU.mult))
                SW(lambda rs=rs: vector.tensor_scalar(sm[:, rs:rs + 1],
                                                      sm[:, NT2:NT2 + 1],
                                                      RC_, None, ALU.add))
                for _ in range(3):
                    SW(lambda rs=rs: vector.tensor_tensor(
                        sm[:, NT1:NT1 + 1], sm[:, rs:rs + 1], sm[:, rs:rs + 1],
                        ALU.mult))
                    SW(lambda: vector.scalar_tensor_tensor(
                        sm[:, NT2:NT2 + 1], sm[:, NT1:NT1 + 1], -0.5,
                        sm[:, LNV:LNV + 1], ALU.mult, ALU.mult))
                    SW(lambda rs=rs: vector.scalar_tensor_tensor(
                        sm[:, rs:rs + 1], sm[:, NT2:NT2 + 1], 1.5, sm[:, rs:rs + 1],
                        ALU.add, ALU.mult))
                SW(lambda c=c, rs=rs: vector.scalar_tensor_tensor(
                    sm[:, LS[c % 2]:LS[c % 2] + 1], sm[:, AMX2[c % 2]:AMX2[c % 2] + 1],
                    1.0 / 127.0, sm[:, rs:rs + 1], ALU.mult, ALU.mult))
                I(lambda c=c, rs=rs: vector.tensor_scalar(
                    bfp[:, XN[c % 2]:XN[c % 2] + 1024],
                    scr[:, CEN[c % 2]:CEN[c % 2] + 1024],
                    sm[:, rs:rs + 1], None, ALU.mult), "v", ("xn", c))

            I(lambda: vector.memset(sm[:, CMG:CMG + 1], MAGIC))
            I(lambda: vector.memset(sm[:, CNMG:CNMG + 1], -MAGIC))
            LN1(0)
            for i in range(8):
                if i % 2 == 0:
                    Wv(vector, "ldw", 80 + 16 * (i // 2 + 1))
                else:
                    Wv(vector, "ldw2", 16 * (i // 2 + 1))
                wst = WST[i % 2]
                stage = scr[:, wst:wst + 1024]
                SW(lambda s=stage: vector.reduce_max(sm[:, AMAX:AMAX + 1], s, AX.X,
                                                     apply_absolute_value=True))
                I(lambda i=i: vector.tensor_scalar_mul(
                    sm[:, RSTL + i:RSTL + i + 1], sm[:, AMAX:AMAX + 1], 1.0 / 127.0))
                SW(lambda: vector.reciprocal(sm[:, RECA:RECA + 1], sm[:, AMAX:AMAX + 1]))
                SW(lambda s=stage: vector.tensor_scalar(
                    scr[:, T1_:T1_ + 1024], s, sm[:, RECA:RECA + 1], 127.0,
                    ALU.mult, ALU.mult))
                SW(lambda: vector.tensor_scalar(q32[:], scr[:, T1_:T1_ + 1024],
                                                MAGIC, -MAGIC, ALU.add, ALU.add))
                I(lambda i=i: vector.tensor_copy(
                    bfp2[:, QRHS_T + i * 1024:QRHS_T + (i + 1) * 1024], q32[:]),
                  "v", ("cast", i))
            W(vector, "t", ("rst", 0))
            I(lambda: vector.tensor_copy(rsrow8[:], p_av[0:8, 0:128]), "v", ("rsrow", 0))
            W(vector, "t", ("rsb", 0))
            I(lambda: vector.tensor_copy(scr[:, RSB:RSB + 1024], p_proj[:, 0:1024]),
              "v", ("rsbe", 0))
            LN2(0)
            LN1(1)

            for i in range(-2, nchunk + 2):
                if 0 <= i + 2 < nchunk and i >= 0:
                    LN1A(i + 2)
                if 0 <= i < nchunk:
                    c = i
                    mask = mc0 if c % 16 == 0 else mstd
                    for g in range(2):
                        W(vector, "a", ("e0" if g == 0 else "e1", c))
                        erv = bass.AP(bfp2, ERAW + g * 2048,
                                      [[BFP2_COLS, 128], [256, 8], [1, 256]])
                        ebv = bass.AP(bfp2, EBF + g * 2048,
                                      [[BFP2_COLS, 128], [256, 8], [1, 256]])
                        mkv = bass.AP(mask, 0, [[256, 128], [0, 8], [1, 256]])
                        I(lambda e=ebv, r=erv, m=mkv: vector.tensor_tensor(e, r, m, ALU.mult),
                          "v", ("e0m" if g == 0 else "e1m", c))
                if 0 <= i + 2 < nchunk and i >= 0:
                    LN1B(i + 2)
                if 0 <= i < nchunk:
                    c = i
                    W(vector, "t", ("ava", c))
                    SW(lambda: vector.reciprocal(
                        sm[:, RSUM:RSUM + 8],
                        bass.AP(p_sc, 64, [[2048, 128], [128, 8]])))
                    aov = bass.AP(bfp2, AO[c % 2], [[BFP2_COLS, 128], [64, 8], [1, 64]])
                    pavv = bass.AP(p_sc, 0, [[2048, 128], [128, 8], [1, 64]])
                    rsv = bass.AP(sm, RSUM, [[64, 128], [1, 8], [0, 64]])
                    I(lambda a=aov, p=pavv, r=rsv: vector.tensor_tensor(a, p, r, ALU.mult),
                      "v", ("ao0", c))
                    W(vector, "t", ("av", c))
                    SW(lambda: vector.reciprocal(
                        sm[:, RSUM + 8:RSUM + 16],
                        bass.AP(p_sc, 1024 + 64, [[2048, 128], [128, 8]])))
                    aov = bass.AP(bfp2, AO[c % 2] + 512,
                                  [[BFP2_COLS, 128], [64, 8], [1, 64]])
                    pavv = bass.AP(p_sc, 1024, [[2048, 128], [128, 8], [1, 64]])
                    rsv = bass.AP(sm, RSUM + 8, [[64, 128], [1, 8], [0, 64]])
                    I(lambda a=aov, p=pavv, r=rsv: vector.tensor_tensor(a, p, r, ALU.mult),
                      "v", ("ao", c))
                if 0 <= i + 2 < nchunk and i >= -1:
                    LN2(i + 2)
                if 0 <= i + 1 < nchunk:
                    c = i + 1
                    W(vector, "t", ("ffn", c))
                    if c >= 1:
                        W(vector, "a", ("silu", c - 1))
                    I(lambda c=c: vector.scalar_tensor_tensor(
                        scr[:, GPRE:GPRE + 512], p_proj[:, 0:512],
                        sm[:, LS[c % 2]:LS[c % 2] + 1], scr[:, RSB:RSB + 512],
                        ALU.mult, ALU.mult), "v", ("ffne0", c))
                    I(lambda c=c: vector.scalar_tensor_tensor(
                        scr[:, GPRE + 512:GPRE + 1024], p_proj[:, 512:1024],
                        sm[:, LS[c % 2]:LS[c % 2] + 1], scr[:, RSB + 512:RSB + 1024],
                        ALU.mult, ALU.mult), "v", ("ffne", c))
                if 0 <= i < nchunk:
                    c = i
                    W(vector, "t", ("op", c))
                    if c >= 2:
                        Wv(vector, "st", 16 * c)
                    SW(lambda c=c: vector.tensor_tensor(scr[:, T1_:T1_ + 1024],
                                                        p_av[:, 0:1024],
                                                        scr[:, GS[c % 2]:GS[c % 2] + 1024],
                                                        ALU.mult))
                    I(lambda c=c: vector.scalar_tensor_tensor(
                        scr[:, OUTS[c % 2]:OUTS[c % 2] + 1024], scr[:, T1_:T1_ + 1024],
                        0.5, scr[:, XS[c % 4]:XS[c % 4] + 1024], ALU.mult, ALU.add),
                      "v", ("comb", c))
                if 0 <= i + 1 < nchunk:
                    c = i + 1
                    W(vector, "a", ("silu", c))
                    I(lambda c=c: vector.scalar_tensor_tensor(
                        scr[:, GS[c % 2]:GS[c % 2] + 1024],
                        scr[:, GS[c % 2]:GS[c % 2] + 1024], 1.0,
                        scr[:, GPRE:GPRE + 1024], ALU.add, ALU.mult), "v", ("g2", c))

        def emit_scalar(scalar):
            def SWS(instr, sname=None, tag=None):
                I(instr, sname or "a", tag)
                if CUR[0].emit:
                    scalar.wait_ge(SEMS[sname or "a"], CUR[0].count[sname or "a"])
            I(lambda: scalar.preload_act_table(AF.Exp))
            for ti in range(1, 8, 2):
                if ti >= 2:
                    W(scalar, "v", ("cast", ti - 2))
                wst = WST[ti % 2]
                src = rhst_h[ti * 128:(ti + 1) * 128, :]
                I(lambda src=src, wst=wst: scalar.dma_start(
                    scr[:, wst:wst + 1024], src), "ldw2", ("ldw2", ti), 16)
            for i in range(-2, nchunk + 2):
                if 0 <= i < nchunk:
                    c = i
                    W(scalar, "t", ("sc0", c))
                    if c >= 1:
                        W(scalar, "v", ("e0m", c - 1))
                    I(lambda: scalar.activation(bfp2[:, ERAW:ERAW + 2048],
                                                p_sc[:, 0:2048], AF.Exp, scale=0.125),
                      "a", ("e0", c))
                if 0 <= i + 1 < nchunk:
                    c = i + 1
                    W(scalar, "t", ("q", c))
                    I(lambda: scalar.copy(bfp[:, QROW:QROW + 1024], p_proj[:, 0:1024]),
                      "a", ("qte", c))
                if 0 <= i < nchunk:
                    c = i
                    W(scalar, "t", ("sc1", c))
                    if c >= 1:
                        W(scalar, "v", ("e1m", c - 1))
                    I(lambda: scalar.activation(bfp2[:, ERAW + 2048:ERAW + 3072],
                                                p_sc[:, 0:1024], AF.Exp, scale=0.125),
                      "a", ("e1h", c))
                    I(lambda: scalar.activation(bfp2[:, ERAW + 3072:ERAW + 4096],
                                                p_sc[:, 1024:2048], AF.Exp, scale=0.125),
                      "a", ("e1", c))
                if 0 <= i + 1 < nchunk:
                    c = i + 1
                    W(scalar, "t", ("v", c))
                    I(lambda c=c: scalar.copy(
                        bass.AP(bfp, VS[c % 3], [[BFP_COLS, 128], [65, 8], [1, 64]]),
                        bass.AP(p_proj, 0, [[1024, 128], [64, 8], [1, 64]])),
                      "a", ("vev0", c))
                    I(lambda c=c: scalar.copy(
                        bass.AP(bfp, VS[c % 3] + 8 * 65,
                                [[BFP_COLS, 128], [65, 8], [1, 64]]),
                        bass.AP(p_proj, 512, [[1024, 128], [64, 8], [1, 64]])),
                      "a", ("vev", c))
                if 0 <= i + 2 < nchunk:
                    c = i + 2
                    W(scalar, "v", ("qrd", c))
                    if c == 0:
                        W(scalar, "v", ("cast", 7))
                    if c >= 2:
                        Wv(scalar, f"trql{c % 2}", 16 * ((c - 2) // 2 + 1))
                    SWS(lambda c=c: scalar.activation(
                        q32[:], scr[:, CEN[c % 2]:CEN[c % 2] + 1024], AF.Copy,
                        scale=sm[:, QRA[c % 2]:QRA[c % 2] + 1], bias=MAGIC))
                    SWS(lambda c=c: scalar.activation(
                        bfp[:, QB[c % 2]:QB[c % 2] + 1024], q32[:], AF.Copy,
                        bias=-MAGIC), "a", ("qbf", c))
                if 0 <= i + 2 < nchunk:
                    c = i + 2
                    I(lambda c=c: scalar.activation(
                        scr[:, CEN[(c + 1) % 2]:CEN[(c + 1) % 2] + 1024],
                        scr[:, CEN[c % 2]:CEN[c % 2] + 1024], AF.Square,
                        accum_out=sm[:, SQS[c % 2]:SQS[c % 2] + 1]), "a", ("sq", c))
                if 0 <= i + 1 < nchunk:
                    c = i + 1
                    W(scalar, "t", ("k", c))
                    I(lambda: scalar.copy(bfp[:, KROW:KROW + 512], p_proj[:, 0:512]),
                      "a", ("kte0", c))
                    I(lambda: scalar.copy(bfp[:, KROW + 512:KROW + 1024],
                                          p_proj[:, 512:1024]), "a", ("kte", c))
                if 0 <= i + 1 < nchunk:
                    c = i + 1
                    Wv(scalar, f"trq{c % 2}", 16 * (c // 2 + 1))
                    I(lambda c=c: scalar.copy(bfp[0:64, QTM[c % 2]:QTM[c % 2] + 1024],
                                              bfp[0:64, QTF:QTF + 1024]))
                    I(lambda c=c: scalar.copy(
                        bfp[64:128, QTM[c % 2] + 1024:QTM[c % 2] + 2048],
                        bfp[64:128, QTF:QTF + 1024]), "a", ("qtm", c))
                if 0 <= i + 1 < nchunk:
                    c = i + 1
                    W(scalar, "v", ("ffne", c))
                    I(lambda c=c: scalar.activation(scr[:, GS[c % 2]:GS[c % 2] + 1024],
                                                    scr[:, GPRE:GPRE + 1024], AF.Tanh,
                                                    scale=0.5), "a", ("silu", c))

        def emit_tensor(tensor):
            W(tensor, "v", ("cast", 7))
            I(lambda: tensor.transpose(p_av[0:8, 0:128], sm[:, RSTL:RSTL + 8], ident[:]),
              "t", ("rst", 0))
            W(tensor, "v", ("rsrow", 0))

            def IMM(mms, sname, tag):
                if CUR[0].emit:
                    last = None
                    for f in mms:
                        r = f()
                        last = r if r is not None else last
                    last.then_inc(SEMS[sname], 1)
                CUR[0].inc(sname, 1, (tag[0], tag[1]))

            IMM([(lambda t=t: tensor.matmul(p_proj[:, t * 128:(t + 1) * 128],
                                            sel[:, t * 128:(t + 1) * 128], rsrow8[:],
                                            start=True, stop=True)) for t in range(8)],
                "t", ("rsb", 0))

            def PROJ(c, name, lhs_t, lhs_base, rhs_t, rhs_base, half1_wait=None):
                mms = []
                for kc in range(8):
                    for half in range(2):
                        if kc == 0 and half == 1 and half1_wait is not None:
                            mms.append(half1_wait)
                        mms.append(lambda kc=kc, half=half: tensor.matmul(
                            p_proj[:, half * 512:(half + 1) * 512],
                            lhs_t[:, lhs_base + kc * 128:lhs_base + kc * 128 + 128],
                            rhs_t[:, rhs_base + kc * 1024 + half * 512:
                                  rhs_base + kc * 1024 + half * 512 + 512],
                            start=(kc == 0), stop=(kc == 7)))
                IMM(mms, "t", (name, c))

            def SCORES(c, g):
                qtm = QTM[c % 2]
                ktc = KT[c % 2]
                ktp = KT[(c + 1) % 2]
                mms = []
                for hh in range(8):
                    h = g * 8 + hh
                    qcol = qtm + (h % 2) * 1024 + (h // 2) * 128
                    tcol = (h // 2) * 128
                    mms.append(lambda hh=hh, qcol=qcol, tcol=tcol: tensor.matmul(
                        p_sc[:, hh * 256:hh * 256 + 128],
                        bfp[:, ktp + tcol:ktp + tcol + 128],
                        bfp[:, qcol:qcol + 128],
                        start=True, stop=True))
                    mms.append(lambda hh=hh, qcol=qcol, tcol=tcol: tensor.matmul(
                        p_sc[:, hh * 256 + 128:hh * 256 + 256],
                        bfp[:, ktc + tcol:ktc + tcol + 128],
                        bfp[:, qcol:qcol + 128],
                        start=True, stop=True))
                IMM(mms, "t", ("sc0" if g == 0 else "sc1", c))

            def AVS(c, grp):
                vc_ = VS[c % 3]
                vp_ = VS[(c + 2) % 3]
                mms = []
                for h in range(grp * 8, grp * 8 + 8):
                    g, hh = divmod(h, 8)
                    e0 = EBF + g * 2048 + hh * 256
                    mms.append(lambda h=h, e0=e0: tensor.matmul(
                        p_sc[:, h * 128:h * 128 + 65], bfp2[:, e0:e0 + 128],
                        bfp[:, vp_ + h * 65:vp_ + h * 65 + 65], start=True, stop=False))
                    mms.append(lambda h=h, e0=e0: tensor.matmul(
                        p_sc[:, h * 128:h * 128 + 65], bfp2[:, e0 + 128:e0 + 256],
                        bfp[:, vc_ + h * 65:vc_ + h * 65 + 65], start=False, stop=True))
                IMM(mms, "t", ("ava" if grp == 0 else "av", c))

            for i in range(-2, nchunk + 2):
                if 0 <= i < nchunk:
                    W(tensor, "a", ("qtm", i))
                    Wv(tensor, f"trk{i % 2}", 16 * (i // 2 + 1))
                    if i >= 1:
                        W(tensor, "v", ("ao", i - 1))
                    SCORES(i, 0)
                if 0 <= i + 1 < nchunk:
                    c = i + 1
                    Wv(tensor, f"trxn{c % 2}", 16 * (c // 2 + 1))
                    if c == 0:
                        Wv(tensor, "lwq", 128)
                    if i >= 0:
                        W(tensor, "v", ("ffne0", i))
                        PROJ(c, "q", bfp, XNT[c % 2], wbf, 0 * 8192,
                             half1_wait=lambda i=i: W(tensor, "v", ("ffne", i)))
                    else:
                        W(tensor, "v", ("rsbe", 0))
                        PROJ(c, "q", bfp, XNT[c % 2], wbf, 0 * 8192)
                if 0 <= i < nchunk:
                    W(tensor, "a", ("e0", i))
                    SCORES(i, 1)
                if 0 <= i + 1 < nchunk:
                    c = i + 1
                    W(tensor, "a", ("qte", c))
                    if c == 0:
                        Wv(tensor, "lwv", 128)
                    PROJ(c, "v", bfp, XNT[c % 2], wbf, 2 * 8192)
                if 0 <= i < nchunk:
                    c = i
                    W(tensor, "a", ("e1h", c))
                    W(tensor, "v", ("e0m", c))
                    AVS(c, 0)
                    W(tensor, "a", ("e1", c))
                    W(tensor, "v", ("e1m", c))
                    AVS(c, 1)
                if 0 <= i + 1 < nchunk:
                    c = i + 1
                    W(tensor, "a", ("vev0", c))
                    if c == 0:
                        Wv(tensor, "lwk", 128)
                    PROJ(c, "k", bfp, XNT[c % 2], wbf, 1 * 8192,
                         half1_wait=lambda c=c: W(tensor, "a", ("vev", c)))
                if 0 <= i < nchunk:
                    c = i
                    Wv(tensor, f"traoa{c % 2}", 16 * (c // 2 + 1))
                    if c >= 1:
                        W(tensor, "v", ("comb", c - 1))
                    if c == 0:
                        Wv(tensor, "lwo", 128)
                    mms = []
                    for kc in range(8):
                        if kc == 4:
                            mms.append(lambda c=c: Wv(
                                tensor, f"trao{c % 2}", 16 * (c // 2 + 1)))
                        for half in range(2):
                            mms.append(lambda kc=kc, half=half, c=c: tensor.matmul(
                                p_av[:, half * 512:(half + 1) * 512],
                                bfp2[:, AOT[c % 2] + kc * 128:AOT[c % 2] + kc * 128 + 128],
                                wbf[:, 3 * 8192 + kc * 1024 + half * 512:
                                    3 * 8192 + kc * 1024 + half * 512 + 512],
                                start=(kc == 0), stop=(kc == 7)))
                    IMM(mms, "t", ("op", c))
                if 0 <= i + 1 < nchunk:
                    c = i + 1
                    W(tensor, "a", ("kte0", c))
                    Wv(tensor, f"trql{c % 2}", 16 * (c // 2 + 1))
                    if c == 0:
                        Wv(tensor, "trs", 128)
                    PROJ(c, "ffn", bfp, QLT[c % 2], qrhs, 0,
                         half1_wait=lambda c=c: W(tensor, "a", ("kte", c)))

        def emit_gpsimd(gpsimd):
            for wi, (wh, semn) in enumerate(zip(w_hs, ("lwq", "lwk", "lwv", "lwo"))):
                for kc in range(8):
                    I(lambda wi=wi, wh=wh, kc=kc: gpsimd.dma_start(
                        wbf[:, wi * 8192 + kc * 1024:wi * 8192 + kc * 1024 + 1024],
                        wh[kc * 128:(kc + 1) * 128, :]), semn, ("lw", wi, kc), 16)

        L1 = Ledger()
        CUR[0] = L1

        class FakeEng:
            def __getattr__(self, name):
                raise RuntimeError("pass-1 must not touch the engine")
        fe = FakeEng()
        emit_sync(fe)
        emit_vector(fe)
        emit_scalar(fe)
        emit_tensor(fe)
        emit_gpsimd(fe)

        L2 = Ledger()
        L2.tag = dict(L1.tag)
        L2.emit = True
        CUR[0] = L2

        @block.sync
        def _(sync):
            emit_sync(sync)

        @block.vector
        def _(vector):
            emit_vector(vector)

        @block.scalar
        def _(scalar):
            emit_scalar(scalar)

        @block.tensor
        def _(tensor):
            emit_tensor(tensor)

        @block.gpsimd
        def _(gpsimd):
            emit_gpsimd(gpsimd)

    return nc


_NC_CACHE = [None]


def kernel(L0_attn_x, L0_attn_Wq, L0_attn_Wk, L0_attn_Wv, L0_attn_Wo, L0_ffn_rhs):
    x = np.ascontiguousarray(np.asarray(L0_attn_x, dtype=np.float32))
    wqt = np.ascontiguousarray(np.asarray(L0_attn_Wq, dtype=np.float32).T)
    wkt = np.ascontiguousarray(np.asarray(L0_attn_Wk, dtype=np.float32).T)
    wvt = np.ascontiguousarray(np.asarray(L0_attn_Wv, dtype=np.float32).T)
    wot = np.ascontiguousarray(np.asarray(L0_attn_Wo, dtype=np.float32).T)
    rhst = np.ascontiguousarray(np.asarray(L0_ffn_rhs, dtype=np.float32).T)

    bf = ml_dtypes.bfloat16
    jj = np.arange(128)[:, None]
    ii = np.arange(128)[None, :]
    m_prev = (jj > ii).astype(np.float32)
    m_cur = (jj <= ii).astype(np.float32)
    mstd = np.concatenate([m_prev, m_cur], axis=1).astype(bf)
    mc0 = np.concatenate([np.zeros((128, 128), np.float32), m_cur], axis=1).astype(bf)
    onesc = np.ones((128, 1), np.float32).astype(bf)
    ident = np.eye(128, dtype=np.float32)
    sel = np.zeros((8, 1024), np.float32)
    for t in range(8):
        sel[t, t * 128:(t + 1) * 128] = 1.0

    if _NC_CACHE[0] is None:
        _NC_CACHE[0] = build()
    nc = _NC_CACHE[0]

    in_maps = []
    for i in range(8):
        in_maps.append({
            "x": x[2 * i:2 * i + 2],
            "wqt": wqt, "wkt": wkt, "wvt": wvt, "wot": wot, "rhst": rhst,
            "mstd": mstd, "mc0": mc0, "onesc": onesc, "ident": ident, "sel": sel,
        })
    res = run_bass_kernel_spmd(nc, in_maps, core_ids=list(range(8)))
    out = np.concatenate([np.asarray(r["y"]) for r in res.results], axis=0)
    return out.reshape(16, 2048, 1024).astype(np.float32)
